# version 05
# baseline (speedup 1.0000x reference)
import os
import numpy as np
import ml_dtypes
from contextlib import ExitStack

import concourse.bass as bass
import concourse.mybir as mybir
from concourse.bass_utils import run_bass_kernel_spmd

F32 = mybir.dt.float32
BF16 = mybir.dt.bfloat16
AF = mybir.ActivationFunctionType
ALU = mybir.AluOpType

D = 4096
S = 4096
NQ = 1024
HFF = 11008
NHT = HFF // 128
ND = D // 128
CH = 512
NCHUNK = S // CH
R = 16
EPS = 1e-5
ISQ = float(1.0 / np.sqrt(128.0))

DEBUG_PHASE = 99
NCHUNK_RUN = NCHUNK
MK_CUT = ""


class _Cut(Exception):
    pass


def ap(t, off, dims):
    return bass.AP(t, off, dims)


class Sem:
    __slots__ = ("h", "n")

    def __init__(self, st, nc, name):
        self.h = st.enter_context(nc.semaphore(name))
        self.n = 0

    def inc(self, ins, k=1):
        ins.then_inc(self.h, k)
        self.n += k
        return ins

    def dma(self, ins):
        return self.inc(ins, 16)


def build_nc():
    nc = bass.Bass("TRN2", target_bir_lowering=False,
                   detect_race_conditions=False)

    xT = nc.dram_tensor("xT", [D, S], F32, kind="ExternalInput")
    x_tm = nc.dram_tensor("x_tm", [S, D], F32, kind="ExternalInput")
    xT_own = nc.dram_tensor("xT_own", [D, NQ], F32, kind="ExternalInput")
    x_own_tm = nc.dram_tensor("x_own_tm", [NQ, D], F32, kind="ExternalInput")
    cosT = nc.dram_tensor("cosT", [128, S], F32, kind="ExternalInput")
    sinT = nc.dram_tensor("sinT", [128, S], F32, kind="ExternalInput")
    cos_q = nc.dram_tensor("cos_q", [128, NQ], F32, kind="ExternalInput")
    sin_q = nc.dram_tensor("sin_q", [128, NQ], F32, kind="ExternalInput")
    WqT = nc.dram_tensor("WqT", [D, D], F32, kind="ExternalInput")
    WkT = nc.dram_tensor("WkT", [D, 1024], F32, kind="ExternalInput")
    WvT = nc.dram_tensor("WvT", [D, 1024], F32, kind="ExternalInput")
    WoT = nc.dram_tensor("WoT", [D, D], F32, kind="ExternalInput")
    w1T = nc.dram_tensor("w1T", [D, HFF], F32, kind="ExternalInput")
    w3T = nc.dram_tensor("w3T", [D, HFF], F32, kind="ExternalInput")
    w2T = nc.dram_tensor("w2T", [HFF, D], F32, kind="ExternalInput")
    A1T = nc.dram_tensor("A1T", [D, R], F32, kind="ExternalInput")
    A3T = nc.dram_tensor("A3T", [D, R], F32, kind="ExternalInput")
    A2T = nc.dram_tensor("A2T", [HFF, R], F32, kind="ExternalInput")
    B1T = nc.dram_tensor("B1T", [R, HFF], F32, kind="ExternalInput")
    B3T = nc.dram_tensor("B3T", [R, HFF], F32, kind="ExternalInput")
    B2T = nc.dram_tensor("B2T", [R, D], F32, kind="ExternalInput")

    y_out = nc.dram_tensor("y_out", [D, NQ], F32, kind="ExternalOutput")

    KT_scr = nc.dram_tensor("KT_scr", [1024, S], BF16, kind="Internal")
    QT_scr = nc.dram_tensor("QT_scr", [128, 32, 2, CH], BF16, kind="Internal")
    AB_scr = nc.dram_tensor("AB_scr", [128, 32, 2, CH], BF16, kind="Internal")
    V_scr = nc.dram_tensor("V_scr", [S, 1024], BF16, kind="Internal")
    Wkb_scr = nc.dram_tensor("Wkb_scr", [D, 1024], BF16, kind="Internal")
    Wvb_scr = nc.dram_tensor("Wvb_scr", [D, 1024], BF16, kind="Internal")
    g_scr = nc.dram_tensor("g_scr", [HFF, NQ], BF16, kind="Internal")
    x1_scr = nc.dram_tensor("x1_scr", [D, NQ], F32, kind="Internal")
    ffn_scr = nc.dram_tensor("ffn_scr", [D, NQ], BF16, kind="Internal")

    dbg = {}
    if DEBUG_PHASE < 99:
        dbg["KT"] = nc.dram_tensor("dbg_KT", [1024, S], BF16, kind="ExternalOutput")
        dbg["V"] = nc.dram_tensor("dbg_V", [S, 1024], BF16, kind="ExternalOutput")
        if DEBUG_PHASE >= 2:
            dbg["QT"] = nc.dram_tensor("dbg_QT", [128, 32 * NQ], BF16, kind="ExternalOutput")
        if DEBUG_PHASE >= 3:
            dbg["AO"] = nc.dram_tensor("dbg_AO", [128, 32 * NQ], BF16, kind="ExternalOutput")
        if DEBUG_PHASE >= 4:
            dbg["X1"] = nc.dram_tensor("dbg_X1", [D, NQ], F32, kind="ExternalOutput")
            dbg["XN2"] = nc.dram_tensor("dbg_XN2", [128, ND * NQ], BF16, kind="ExternalOutput")
        if DEBUG_PHASE >= 5:
            dbg["G"] = nc.dram_tensor("dbg_G", [HFF, NQ], BF16, kind="ExternalOutput")
        if DEBUG_PHASE >= 6:
            dbg["FF"] = nc.dram_tensor("dbg_FF", [D, NQ], BF16, kind="ExternalOutput")

    te, ve, sc, gp, sy = nc.tensor, nc.vector, nc.scalar, nc.gpsimd, nc.sync

    def maybe_cut(tag):
        if MK_CUT == tag:
            raise _Cut()

    st = ExitStack()
    E = st.enter_context
    qt_stack = ExitStack()
    ab_stack = ExitStack()
    xn2_stack = ExitStack()
    try:
      try:
        ident_b = E(nc.sbuf_tensor("ident_b", [128, 128], BF16))
        ident_f = E(nc.sbuf_tensor("ident_f", [128, 128], F32))
        ones_row = E(nc.sbuf_tensor("ones_row", [1, 128], F32))
        ones_col = E(nc.sbuf_tensor("ones_col", [128, 1], BF16))
        cM01 = E(nc.sbuf_tensor("cM01", [128, 1536], BF16))
        c_isq = E(nc.sbuf_tensor("c_isq", [128, 1], F32))
        c_neg5 = E(nc.sbuf_tensor("c_neg5", [128, 1], F32))

        dg = Sem(st, nc, "dg")
        dy = Sem(st, nc, "dy")
        dc = Sem(st, nc, "dc")
        mt = Sem(st, nc, "mt")
        mv = Sem(st, nc, "mv")
        ma = Sem(st, nc, "ma")
        mg = Sem(st, nc, "mg")

        mg.inc(gp.memset(ident_b[:], 0.0))
        gp.wait_ge(mg.h, mg.n)
        mg.inc(gp.affine_select(out=ident_b[:], in_=ident_b[:], compare_op=ALU.not_equal,
                                fill=1.0, base=0, pattern=[[-1, 128]], channel_multiplier=1))
        gp.wait_ge(mg.h, mg.n)
        mg.inc(gp.memset(ident_f[:], 0.0))
        gp.wait_ge(mg.h, mg.n)
        mg.inc(gp.affine_select(out=ident_f[:], in_=ident_f[:], compare_op=ALU.not_equal,
                                fill=1.0, base=0, pattern=[[-1, 128]], channel_multiplier=1))
        gp.wait_ge(mg.h, mg.n)
        mg.inc(gp.memset(ones_row[:], 1.0))
        gp.wait_ge(mg.h, mg.n)
        mg.inc(gp.memset(ones_col[:], 1.0))
        gp.wait_ge(mg.h, mg.n)
        mg.inc(gp.memset(cM01[:], 0.0))
        gp.wait_ge(mg.h, mg.n)
        mg.inc(gp.affine_select(out=cM01[:], in_=cM01[:], compare_op=ALU.is_gt,
                                fill=1.0, base=896, pattern=[[-1, 1536]], channel_multiplier=1))
        gp.wait_ge(mg.h, mg.n)
        mg.inc(gp.memset(c_isq[:], ISQ))
        gp.wait_ge(mg.h, mg.n)
        mg.inc(gp.memset(c_neg5[:], -0.5))
        te.wait_ge(mg.h, mg.n)
        ve.wait_ge(mg.h, mg.n)
        sc.wait_ge(mg.h, mg.n)

        def fence():
            for eng in (te, ve, sc, sy, gp):
                eng.wait_ge(mt.h, mt.n)
                eng.wait_ge(mv.h, mv.n)
                eng.wait_ge(ma.h, ma.n)
                eng.wait_ge(dy.h, dy.n)
                eng.wait_ge(dg.h, dg.n)
                eng.wait_ge(dc.h, dc.n)


        def istd_rows(row_ap):
            sc.wait_ge(mv.h, mv.n)
            sc.activation(row_ap, row_ap, AF.Ln)
            ma.inc(sc.activation(row_ap, row_ap, AF.Exp, scale=c_neg5[0:1, :]))

        def emit_stats_rows(src_tm, ntiles, q0_list, bufs, reuse, row_at=None):
            xtm2, bn6, bn2, mcol, icol, ps_s, mrow, irow = bufs
            nb = len(xtm2)
            for t0_ in range(ntiles):
                t = t0_ if row_at is None else row_at
                tb = t % nb
                sy.wait_ge(mv.h, reuse["xtm"][tb])
                q0_ = q0_list[t0_]
                dy.dma(sy.dma_start(xtm2[tb][:], src_tm[q0_:q0_ + 128, :]))
                ve.wait_ge(dy.h, dy.n)
                for i in range(8):
                    ins = ve.bn_stats(bn6[:, 6 * i:6 * (i + 1)],
                                      xtm2[tb][:, 512 * i:512 * (i + 1)])
                mv.inc(ins)
                ve.wait_ge(mv.h, mv.n)
                mv.inc(ve.bn_aggr(bn2[:], bn6[:]))
                reuse["xtm"][tb] = mv.n
                ve.wait_ge(mv.h, mv.n)
                ve.wait_ge(mt.h, reuse["ps_s_te"])
                ve.tensor_copy(mcol[:], bn2[:, 0:1])
                mv.inc(ve.tensor_scalar(icol[:], bn2[:, 1:2], EPS, None, ALU.add))
                sc.wait_ge(mv.h, mv.n)
                sc.activation(icol[:], icol[:], AF.Ln)
                ma.inc(sc.activation(icol[:], icol[:], AF.Exp, scale=c_neg5[:]))
                te.wait_ge(ma.h, ma.n)
                te.wait_ge(mv.h, reuse["ps_s_ve"])
                mt.inc(te.transpose(ps_s[0:1, 0:128], mcol[:], ident_f[:]))
                mt.inc(te.transpose(ps_s[0:1, 128:256], icol[:], ident_f[:]))
                reuse["ps_s_te"] = mt.n
                ve.wait_ge(mt.h, mt.n)
                ve.tensor_copy(mrow[0:1, 128 * t:128 * (t + 1)], ps_s[0:1, 0:128])
                mv.inc(ve.tensor_copy(irow[0:1, 128 * t:128 * (t + 1)], ps_s[0:1, 128:256]))
                reuse["ps_s_ve"] = mv.n

        def emit_bcast(ps_dst, col0, row_src, width):
            mt.inc(te.matmul(ps_dst[:, col0:col0 + width], ones_row[:],
                             row_src, start=True, stop=True))

        with ExitStack() as pa:
            Ep = pa.enter_context
            xp = [Ep(nc.sbuf_tensor("xp0", [128, ND, 128], F32))]
            xf2 = [Ep(nc.sbuf_tensor(f"xf{i}", [128, ND, 128], F32)) for i in range(2)]
            xtm2 = [Ep(nc.sbuf_tensor(f"xtm{i}", [128, D], F32)) for i in range(2)]
            xn2b = [Ep(nc.sbuf_tensor(f"xn{i}", [128, ND, CH], BF16)) for i in range(2)]
            bn6 = Ep(nc.sbuf_tensor("bn6", [128, 48], F32))
            bn2 = Ep(nc.sbuf_tensor("bn2", [128, 2], F32))
            mcol = Ep(nc.sbuf_tensor("mcol", [128, 1], F32))
            icol = Ep(nc.sbuf_tensor("icol", [128, 1], F32))
            mrow = Ep(nc.sbuf_tensor("mrow", [1, CH], F32))
            irow = Ep(nc.sbuf_tensor("irow", [1, CH], F32))
            wkp = [Ep(nc.sbuf_tensor(f"wkp{i}", [128, ND, 128], BF16)) for i in range(2)]
            cosc = Ep(nc.sbuf_tensor("cosc", [128, CH], F32))
            sinc = Ep(nc.sbuf_tensor("sinc", [128, CH], F32))
            kst = Ep(nc.sbuf_tensor("kst", [128, 8, CH], BF16))
            vst = Ep(nc.sbuf_tensor("vst", [128, 8, CH], BF16))
            vtm = Ep(nc.sbuf_tensor("vtm", [128, 4, 1024], BF16))
            rt1 = Ep(nc.sbuf_tensor("rt1", [64, CH], F32))
            rt2 = Ep(nc.sbuf_tensor("rt2", [64, CH], F32))
            rt3 = Ep(nc.sbuf_tensor("rt3", [64, CH], F32))
            rt4 = Ep(nc.sbuf_tensor("rt4", [64, CH], F32))
            ps_b = Ep(nc.psum_tensor("ps_b", [128, 2 * CH], F32))
            ps_s = Ep(nc.psum_tensor("ps_s", [1, 512], F32))
            ps_kv = [Ep(nc.psum_tensor(f"ps_kv{i}", [128, CH], F32)) for i in range(3)]
            ps_tr = [Ep(nc.psum_tensor(f"ps_tr{i}", [128, 1024], BF16)) for i in range(2)]

            stat_bufs = (xtm2, bn6, bn2, mcol, icol, ps_s, mrow, irow)
            reuse = {"xtm": {0: 0, 1: 0}, "ps_s_te": 0, "ps_s_ve": 0}
            xp_free = {0: 0}
            wkp_free_mt = {0: 0, 1: 0}
            wkp_free_dy = {0: 0, 1: 0}
            kv_bank_free = {0: 0, 1: 0, 2: 0}
            vtr_free = {0: 0, 1: 0}
            ps_b_busy = 0
            cos_busy = 0
            spool_done = 0

            XN_MV = {0: 0, 1: 0}

            def make_pieces(c):
                pieces = []
                state = {}
                for t in range(4):
                    def stats_piece(t=t, c=c):
                        emit_stats_rows(x_tm, 1, [CH * c + 128 * t], stat_bufs, reuse,
                                        row_at=t)
                        state["rows_mv"] = mv.n
                    pieces.append(stats_piece)

                def bcast_piece(c=c):
                    te.wait_ge(mv.h, max(state["rows_mv"], XN_MV[(c + 1) % 2]))
                    emit_bcast(ps_b, 0, mrow[0:1, :], CH)
                    emit_bcast(ps_b, CH, irow[0:1, :], CH)
                    state["bc_mt"] = mt.n
                pieces.append(bcast_piece)
                for t in range(4):
                    def center_piece(t=t, c=c):
                        xnb = xn2b[c % 2]
                        tb = 0
                        gp.wait_ge(mv.h, xp_free[tb])
                        dg.dma(gp.dma_start(
                            xp[tb][:],
                            ap(xT, CH * c + 128 * t, [[S, 128], [128 * S, ND], [1, 128]])))
                        ve.wait_ge(dg.h, dg.n)
                        ve.wait_ge(mt.h, state["bc_mt"])
                        mb_ap = ap(ps_b, 128 * t, [[2 * CH, 128], [0, ND], [1, 128]])
                        ib_ap = ap(ps_b, CH + 128 * t, [[2 * CH, 128], [0, ND], [1, 128]])
                        mv.inc(ve.tensor_tensor(xf2[tb][:], xp[tb][:], mb_ap, ALU.subtract))
                        ve.wait_ge(mv.h, mv.n)
                        mv.inc(ve.tensor_tensor(xnb[:, :, 128 * t:128 * (t + 1)],
                                                xf2[tb][:], ib_ap, ALU.mult))
                        xp_free[tb] = mv.n
                        XN_MV[c % 2] = mv.n
                    pieces.append(center_piece)
                return pieces

            for pc in make_pieces(0):
                pc()
            pending = []
            for c in range(NCHUNK_RUN):
                xnb = xn2b[c % 2]
                if c + 1 < NCHUNK_RUN:
                    pending = make_pieces(c + 1)
                sc.wait_ge(mv.h, cos_busy)
                dc.dma(sc.dma_start(cosc[:], cosT[:, CH * c:CH * (c + 1)]))
                dc.dma(sc.dma_start(sinc[:], sinT[:, CH * c:CH * (c + 1)]))
                COS_DC = dc.n

                for kv in range(2):
                    wsrc, wspool = (WkT, Wkb_scr) if kv == 0 else (WvT, Wvb_scr)
                    for g in range(8):
                        pidx = 16 * c + 8 * kv + g
                        pb = pidx % 2
                        wap_dims = [[1024, 128], [128 * 1024, ND], [1, 128]]
                        if c == 0:
                            gp.wait_ge(mt.h, wkp_free_mt[pb])
                            gp.wait_ge(dy.h, wkp_free_dy[pb])
                            dg.dma(gp.dma_start(wkp[pb][:], ap(wsrc, 128 * g, wap_dims)))
                            PANEL_D = ("dg", dg.n)
                            sy.wait_ge(dg.h, dg.n)
                            dy.dma(sy.dma_start(ap(wspool, 128 * g, wap_dims), wkp[pb][:]))
                            wkp_free_dy[pb] = dy.n
                        else:
                            sy.wait_ge(mt.h, wkp_free_mt[pb])
                            dy.dma(sy.dma_start(wkp[pb][:], ap(wspool, 128 * g, wap_dims)))
                            PANEL_D = ("dy", dy.n)
                        bank = pidx % 3
                        te.wait_ge((dg if PANEL_D[0] == "dg" else dy).h, PANEL_D[1])
                        te.wait_ge(mv.h, max(XN_MV[c % 2], kv_bank_free[bank]))
                        ins = None
                        for a in range(ND):
                            ins = te.matmul(ps_kv[bank][:, :], wkp[pb][:, a, :],
                                            xnb[:, a, :],
                                            start=(a == 0), stop=(a == ND - 1))
                        mt.inc(ins)
                        wkp_free_mt[pb] = mt.n
                        ve.wait_ge(mt.h, mt.n)
                        if kv == 0:
                            ve.wait_ge(dc.h, COS_DC)
                            ve.wait_ge(dy.h, spool_done)
                            ve.tensor_tensor(rt1[:], ps_kv[bank][0:64, :],
                                             cosc[0:64, :], ALU.mult)
                            ve.tensor_tensor(rt2[:], ps_kv[bank][64:128, :],
                                             sinc[64:128, :], ALU.mult)
                            ve.tensor_sub(kst[0:64, g, :], rt1[:], rt2[:])
                            ve.tensor_tensor(rt3[:], ps_kv[bank][64:128, :],
                                             cosc[64:128, :], ALU.mult)
                            ve.tensor_tensor(rt4[:], ps_kv[bank][0:64, :],
                                             sinc[0:64, :], ALU.mult)
                            mv.inc(ve.tensor_add(kst[64:128, g, :], rt3[:], rt4[:]))
                        else:
                            ve.wait_ge(dy.h, spool_done)
                            mv.inc(ve.tensor_copy(vst[:, g, :], ps_kv[bank][:, :]))
                        kv_bank_free[bank] = mv.n
                        if pending:
                            pending.pop(0)()
                cos_busy = mv.n
                while pending:
                    pending.pop(0)()
                VST_MV = mv.n
                for g in range(8):
                    for t4 in range(4):
                        slot = (4 * g + t4) % 2
                        te.wait_ge(mv.h, max(VST_MV, vtr_free[slot]))
                        mt.inc(te.transpose(ps_tr[slot][:, 0:128],
                                            vst[:, g, 128 * t4:128 * (t4 + 1)],
                                            ident_b[:]))
                        ve.wait_ge(mt.h, mt.n)
                        mv.inc(ve.tensor_copy(vtm[:, t4, 128 * g:128 * (g + 1)],
                                              ps_tr[slot][:, 0:128]))
                        vtr_free[slot] = mv.n
                sy.wait_ge(mv.h, mv.n)
                dy.dma(sy.dma_start(
                    ap(KT_scr, CH * c, [[S, 128], [128 * S, 8], [1, CH]]), kst[:]))
                dy.dma(sy.dma_start(
                    ap(V_scr, 1024 * CH * c, [[1024, 128], [128 * 1024, 4], [1, 1024]]),
                    vtm[:]))
                spool_done = dy.n

        if DEBUG_PHASE < 99:
            sy.wait_ge(dy.h, dy.n)
            dy.dma(sy.dma_start(dbg["KT"][:, :], KT_scr[:, :]))
            dy.dma(sy.dma_start(dbg["V"][:, :], V_scr[:, :]))
        if DEBUG_PHASE == 1:
            sy.wait_ge(dy.h, dy.n)
            sy.wait_ge(dg.h, dg.n)
            sy.wait_ge(dc.h, dc.n)
            return nc, dbg

        fence()
        QT_A = qt_stack.enter_context(nc.sbuf_tensor("QT_A", [128, 32, CH], BF16, side="right"))
        QT_B = qt_stack.enter_context(nc.sbuf_tensor("QT_B", [128, 32, CH], BF16, side="right"))
        with ExitStack() as pq:
            Ep = pq.enter_context
            xp = [Ep(nc.sbuf_tensor("qxp0", [128, ND, 128], F32))]
            xtm2 = [Ep(nc.sbuf_tensor("qxtm0", [128, D], F32))]
            xnq = Ep(nc.sbuf_tensor("xnq", [128, ND, NQ], BF16))
            bn6 = Ep(nc.sbuf_tensor("qbn6", [128, 48], F32))
            bn2 = Ep(nc.sbuf_tensor("qbn2", [128, 2], F32))
            mcol = Ep(nc.sbuf_tensor("qmcol", [128, 1], F32))
            icol = Ep(nc.sbuf_tensor("qicol", [128, 1], F32))
            mrow = Ep(nc.sbuf_tensor("qmrow", [1, NQ], F32))
            irow = Ep(nc.sbuf_tensor("qirow", [1, NQ], F32))
            wqp = [Ep(nc.sbuf_tensor(f"wqp{i}", [128, ND, 128], BF16)) for i in range(2)]
            cq = Ep(nc.sbuf_tensor("cq", [128, NQ], F32))
            sq = Ep(nc.sbuf_tensor("sq", [128, NQ], F32))
            rt1 = Ep(nc.sbuf_tensor("qrt1", [64, CH], F32))
            rt2 = Ep(nc.sbuf_tensor("qrt2", [64, CH], F32))
            rt3 = Ep(nc.sbuf_tensor("qrt3", [64, CH], F32))
            rt4 = Ep(nc.sbuf_tensor("qrt4", [64, CH], F32))
            ps_b = Ep(nc.psum_tensor("qps_b", [128, 2 * CH], F32))
            ps_s = Ep(nc.psum_tensor("qps_s", [1, 256], F32))
            ps_q = [Ep(nc.psum_tensor(f"ps_q{i}", [128, CH], F32)) for i in range(4)]

            stat_bufs = (xtm2, bn6, bn2, mcol, icol, ps_s, mrow, irow)
            reuse = {"xtm": {0: 0, 1: 0}, "ps_s_te": 0, "ps_s_ve": 0}
            emit_stats_rows(x_own_tm, 8, [128 * t for t in range(8)], stat_bufs, reuse)
            dc.dma(sc.dma_start(cq[:], cos_q[:, :]))
            dc.dma(sc.dma_start(sq[:], sin_q[:, :]))
            xp_free = {0: 0}
            ps_b_busy = 0
            for half in range(2):
                te.wait_ge(mv.h, max(mv.n, ps_b_busy))
                emit_bcast(ps_b, 0, mrow[0:1, CH * half:CH * (half + 1)], CH)
                emit_bcast(ps_b, CH, irow[0:1, CH * half:CH * (half + 1)], CH)
                BC_MT = mt.n
                for t4 in range(4):
                    t = 4 * half + t4
                    gp.wait_ge(mv.h, xp_free[0])
                    dg.dma(gp.dma_start(
                        xp[0][:],
                        ap(xT_own, 128 * t, [[NQ, 128], [128 * NQ, ND], [1, 128]])))
                    ve.wait_ge(dg.h, dg.n)
                    ve.wait_ge(mt.h, BC_MT)
                    mb_ap = ap(ps_b, 128 * t4, [[2 * CH, 128], [0, ND], [1, 128]])
                    ib_ap = ap(ps_b, CH + 128 * t4, [[2 * CH, 128], [0, ND], [1, 128]])
                    ve.tensor_tensor(xp[0][:], xp[0][:], mb_ap, ALU.subtract)
                    mv.inc(ve.tensor_tensor(xnq[:, :, 128 * t:128 * (t + 1)], xp[0][:],
                                            ib_ap, ALU.mult))
                    xp_free[0] = mv.n
                ps_b_busy = mv.n
            XNQ_MV = mv.n
            wqp_free = {0: 0, 1: 0}
            q_bank_free = {0: 0, 1: 0, 2: 0, 3: 0}
            for h in range(32):
                pb = h % 2
                gp.wait_ge(mt.h, wqp_free[pb])
                dg.dma(gp.dma_start(
                    wqp[pb][:], ap(WqT, 128 * h, [[D, 128], [128 * D, ND], [1, 128]])))
                te.wait_ge(dg.h, dg.n)
                te.wait_ge(dc.h, dc.n)
                for c2 in range(2):
                    bank = (2 * h + c2) % 4
                    te.wait_ge(mv.h, max(XNQ_MV, q_bank_free[bank]))
                    ins = None
                    for a in range(ND):
                        ins = te.matmul(ps_q[bank][:, :], wqp[pb][:, a, :],
                                        xnq[:, a, CH * c2:CH * (c2 + 1)],
                                        start=(a == 0), stop=(a == ND - 1))
                    mt.inc(ins)
                    if c2 == 1:
                        wqp_free[pb] = mt.n
                    dstq = QT_A if c2 == 0 else QT_B
                    cs0 = CH * c2
                    ve.wait_ge(mt.h, mt.n)
                    ve.tensor_tensor(rt1[:], ps_q[bank][0:64, :],
                                     cq[0:64, cs0:cs0 + CH], ALU.mult)
                    ve.tensor_tensor(rt2[:], ps_q[bank][64:128, :],
                                     sq[64:128, cs0:cs0 + CH], ALU.mult)
                    ve.tensor_sub(dstq[0:64, h, :], rt1[:], rt2[:])
                    ve.tensor_tensor(rt3[:], ps_q[bank][64:128, :],
                                     cq[64:128, cs0:cs0 + CH], ALU.mult)
                    ve.tensor_tensor(rt4[:], ps_q[bank][0:64, :],
                                     sq[0:64, cs0:cs0 + CH], ALU.mult)
                    mv.inc(ve.tensor_add(dstq[64:128, h, :], rt3[:], rt4[:]))
                    q_bank_free[bank] = mv.n

        sy.wait_ge(mv.h, mv.n)
        dy.dma(sy.dma_start(
            ap(QT_scr, 0, [[32 * 2 * CH, 128], [2 * CH, 32], [1, CH]]), QT_A[:]))
        dy.dma(sy.dma_start(
            ap(QT_scr, CH, [[32 * 2 * CH, 128], [2 * CH, 32], [1, CH]]), QT_B[:]))
        QT_SPOOL_DY = dy.n
        if DEBUG_PHASE < 99 and DEBUG_PHASE >= 2:
            sy.wait_ge(mv.h, mv.n)
            dy.dma(sy.dma_start(
                ap(dbg["QT"], 0, [[32 * NQ, 128], [NQ, 32], [1, CH]]), QT_A[:]))
            dy.dma(sy.dma_start(
                ap(dbg["QT"], CH, [[32 * NQ, 128], [NQ, 32], [1, CH]]), QT_B[:]))
        if DEBUG_PHASE == 2:
            sy.wait_ge(dy.h, dy.n)
            sy.wait_ge(dg.h, dg.n)
            sy.wait_ge(dc.h, dc.n)
            return nc, dbg

        fence()
        sy.wait_ge(dy.h, QT_SPOOL_DY)
        qt_stack.close()
        with ExitStack() as pt:
            Ep = pt.enter_context
            kh = Ep(nc.sbuf_tensor("kh", [128, 4, S], BF16))
            vh = Ep(nc.sbuf_tensor("vh", [128, 32, CH], BF16))
            qh = Ep(nc.sbuf_tensor("qh", [128, 16, 2, CH], BF16))
            abh = Ep(nc.sbuf_tensor("abh", [128, 2, 16, CH], BF16))
            linv = Ep(nc.sbuf_tensor("linv", [1, CH], F32))
            lb32 = Ep(nc.sbuf_tensor("lb32", [128, CH], F32))
            psb = [Ep(nc.sbuf_tensor(f"psb{i}", [128, 2 * CH], BF16)) for i in range(2)]
            ps_S = [Ep(nc.psum_tensor(f"ps_S{i}", [128, 2 * CH], F32)) for i in range(2)]
            ps_A = [Ep(nc.psum_tensor(f"ps_A{i}", [128, CH], F32)) for i in range(2)]
            ps_L = Ep(nc.psum_tensor("ps_L", [1, CH], F32))
            ps_B = Ep(nc.psum_tensor("ps_B", [128, CH], F32))

            sw_engines = [te, ve, sc, sy]

            hh_compute_done = 0
            for hh in range(2):
                sy.wait_ge(mv.h, hh_compute_done)
                dy.dma(sy.dma_start(
                    kh[:], ap(KT_scr, 512 * hh * S, [[S, 128], [128 * S, 4], [1, S]])))
                dy.dma(sy.dma_start(
                    vh[:], ap(V_scr, 512 * hh, [[1024, 128], [128 * 1024, 32], [1, CH]])))
                dy.dma(sy.dma_start(
                    qh[:], ap(QT_scr, 16 * hh * 2 * CH,
                              [[32 * 2 * CH, 128], [2 * CH, 16], [1, 2 * CH]])))
                HH_DY = dy.n
                te.wait_ge(dy.h, HH_DY)
                ve.wait_ge(dy.h, HH_DY)
                sc.wait_ge(dy.h, HH_DY)
                idx = [eng.compute_val(eng.partition_id() % 4) for eng in sw_engines]

                sem_snap = None
                body_end = None
                for j in nc.Switch(engines=sw_engines, index=idx, n=4):
                    if sem_snap is None:
                        sem_snap = (mt.n, mv.n, ma.n)
                    else:
                        mt.n, mv.n, ma.n = sem_snap
                    psb_free = {0: 0, 1: 0}
                    psS_free = {0: 0, 1: 0}
                    psA_free = {0: 0, 1: 0}
                    psL_free = 0
                    psB_free = 0
                    for qsel in range(2):
                        npan = (j + 1) if qsel == 0 else (8 - j)
                        diag_p = j if qsel == 0 else 7 - j
                        for hl in range(16):
                            gl = hl // 4
                            ab = (16 * qsel + hl) % 2
                            for p in range(npan):
                                for pr in range(2):
                                    te.wait_ge(ma.h, psS_free[pr])
                                    for k2 in range(2):
                                        kt = 2 * pr + k2
                                        mt.inc(te.matmul(
                                            ps_S[pr][:, CH * k2:CH * (k2 + 1)],
                                            kh[:, gl, 512 * p + 128 * kt:512 * p + 128 * (kt + 1)],
                                            qh[:, hl, qsel, :], start=True, stop=True))
                                    sc.wait_ge(mt.h, mt.n)
                                    sc.wait_ge(mt.h, psb_free[pr])
                                    ma.inc(sc.activation(psb[pr][:, :], ps_S[pr][:, :],
                                                         AF.Exp, scale=c_isq[:]))
                                    psS_free[pr] = ma.n
                                if p == diag_p:
                                    ve.wait_ge(ma.h, ma.n)
                                    for kt in range(4):
                                        pr, k2 = divmod(kt, 2)
                                        c0 = 896 - 128 * kt
                                        mv.inc(ve.tensor_tensor(
                                            psb[pr][:, CH * k2:CH * (k2 + 1)],
                                            psb[pr][:, CH * k2:CH * (k2 + 1)],
                                            cM01[:, c0:c0 + CH], ALU.mult))
                                te.wait_ge(ma.h, ma.n)
                                if p == diag_p:
                                    te.wait_ge(mv.h, mv.n)
                                if p == 0:
                                    te.wait_ge(mv.h, psA_free[ab])
                                    te.wait_ge(mv.h, psL_free)
                                ins = None
                                for kt in range(4):
                                    pr, k2 = divmod(kt, 2)
                                    ins = te.matmul(
                                        ps_A[ab][:, :],
                                        vh[:, 4 * p + kt, 128 * gl:128 * (gl + 1)],
                                        psb[pr][:, CH * k2:CH * (k2 + 1)],
                                        start=(p == 0 and kt == 0),
                                        stop=(p == npan - 1 and kt == 3))
                                for kt in range(4):
                                    pr, k2 = divmod(kt, 2)
                                    ins = te.matmul(
                                        ps_L[0:1, :],
                                        ones_col[:],
                                        psb[pr][:, CH * k2:CH * (k2 + 1)],
                                        start=(p == 0 and kt == 0),
                                        stop=(p == npan - 1 and kt == 3))
                                mt.inc(ins)
                                psb_free[0] = mt.n
                                psb_free[1] = mt.n
                            AV_MT = mt.n
                            ve.wait_ge(mt.h, AV_MT)
                            mv.inc(ve.reciprocal(linv[0:1, :], ps_L[0:1, :]))
                            psL_free = mv.n
                            te.wait_ge(mv.h, mv.n)
                            te.wait_ge(mv.h, psB_free)
                            mt.inc(te.matmul(ps_B[:, :], ones_row[:], linv[0:1, :],
                                             start=True, stop=True))
                            ve.wait_ge(mt.h, mt.n)
                            mv.inc(ve.tensor_copy(lb32[:], ps_B[:, :]))
                            psB_free = mv.n
                            ve.wait_ge(mv.h, mv.n)
                            mv.inc(ve.tensor_tensor(abh[:, qsel, hl, :], ps_A[ab][:, :],
                                                    lb32[:], ALU.mult))
                            psA_free[ab] = mv.n
                    be = (mt.n, mv.n, ma.n)
                    assert body_end is None or body_end == be, (body_end, be)
                    body_end = be
                hh_compute_done = mv.n
                sy.wait_ge(mv.h, mv.n)
                for qsel in range(2):
                    dy.dma(sy.dma_start(
                        ap(AB_scr, 16 * hh * 2 * CH + CH * qsel,
                           [[32 * 2 * CH, 128], [2 * CH, 16], [1, CH]]),
                        abh[:, qsel, :, :]))
            AB_SPOOL_DY = dy.n

        if DEBUG_PHASE < 99 and DEBUG_PHASE >= 3:
            sy.wait_ge(dy.h, AB_SPOOL_DY)
            dy.dma(sy.dma_start(dbg["AO"][:, :],
                                ap(AB_scr, 0, [[32 * 2 * CH, 128], [1, 32 * 2 * CH]])))
        if DEBUG_PHASE == 3:
            sy.wait_ge(dy.h, dy.n)
            sy.wait_ge(dg.h, dg.n)
            sy.wait_ge(dc.h, dc.n)
            return nc, dbg

        qt_stack.close()

        fence()
        with ExitStack() as pw:
            Ep = pw.enter_context
            wop = [Ep(nc.sbuf_tensor(f"wop{i}", [128, ND, 128], BF16, side="right")) for i in range(2)]
            ysb = Ep(nc.sbuf_tensor("ysb", [128, 32, NQ], BF16, side="right"))
            ab16f = ab_stack.enter_context(nc.sbuf_tensor("ab16f", [128, 32, 2, CH], BF16))
            sy.wait_ge(dy.h, AB_SPOOL_DY)
            dy.dma(sy.dma_start(ab16f[:], AB_scr[:]))
            ABF_DY = dy.n
            with ExitStack() as pw1:
                ps_y = [pw1.enter_context(nc.psum_tensor(f"ps_y{i}", [128, CH], F32))
                        for i in range(4)]
                wop_free = {0: 0, 1: 0}
                y_bank_free = {0: 0, 1: 0, 2: 0, 3: 0}
                for o in range(32):
                    pb = o % 2
                    gp.wait_ge(mt.h, wop_free[pb])
                    dg.dma(gp.dma_start(
                        wop[pb][:],
                        ap(WoT, 128 * o, [[D, 128], [128 * D, ND], [1, 128]])))
                    te.wait_ge(dg.h, dg.n)
                    te.wait_ge(dy.h, ABF_DY)
                    for c2 in range(2):
                        bank = (2 * o + c2) % 4
                        te.wait_ge(mv.h, y_bank_free[bank])
                        ins = None
                        for a in range(ND):
                            ins = te.matmul(ps_y[bank][:, :], wop[pb][:, a, :],
                                            ab16f[:, a, c2, :],
                                            start=(a == 0), stop=(a == ND - 1))
                        mt.inc(ins)
                        if c2 == 1:
                            wop_free[pb] = mt.n
                        ve.wait_ge(mt.h, mt.n)
                        mv.inc(ve.tensor_copy(ysb[:, o, CH * c2:CH * (c2 + 1)],
                                              ps_y[bank][:, :]))
                        y_bank_free[bank] = mv.n
            ab_stack.close()
            fence()
            xn2 = xn2_stack.enter_context(nc.sbuf_tensor("xn2", [128, ND, NQ], BF16))
            YS_MV = mv.n

            sqb = [Ep(nc.sbuf_tensor(f"sqb{i}", [128, CH], BF16)) for i in range(2)]
            yrow = Ep(nc.sbuf_tensor("yrow", [1, NQ], F32))
            yirow = Ep(nc.sbuf_tensor("yirow", [1, NQ], F32))
            m2r = Ep(nc.sbuf_tensor("m2r", [1, NQ], F32))
            xg = [Ep(nc.sbuf_tensor(f"xg{i}", [128, 4, CH], F32)) for i in range(2)]
            x1p = [Ep(nc.sbuf_tensor(f"x1p{i}", [128, CH], F32)) for i in range(2)]
            tf2 = [Ep(nc.sbuf_tensor(f"tf{i}", [128, CH], F32)) for i in range(2)]
            x1b = Ep(nc.sbuf_tensor("x1b", [128, CH], BF16))
            ps_st = Ep(nc.psum_tensor("ps_st", [1, 2 * NQ], F32))
            ps_b2 = Ep(nc.psum_tensor("ps_b2", [128, 2 * CH], F32))

            sq_free = {0: 0, 1: 0}
            for a in range(ND):
                for c2 in range(2):
                    sb = (2 * a + c2) % 2
                    sc.wait_ge(mt.h, sq_free[sb])
                    sc.wait_ge(mv.h, YS_MV)
                    ma.inc(sc.activation(sqb[sb][:], ysb[:, a, CH * c2:CH * (c2 + 1)],
                                         AF.Square))
                    te.wait_ge(ma.h, ma.n)
                    te.matmul(ps_st[0:1, CH * c2:CH * (c2 + 1)], ones_col[:],
                              ysb[:, a, CH * c2:CH * (c2 + 1)],
                              start=(a == 0), stop=(a == ND - 1))
                    mt.inc(te.matmul(ps_st[0:1, NQ + CH * c2:NQ + CH * (c2 + 1)],
                                     ones_col[:], sqb[sb][:],
                                     start=(a == 0), stop=(a == ND - 1)))
                    sq_free[sb] = mt.n
            ve.wait_ge(mt.h, mt.n)
            mv.inc(ve.tensor_scalar(yrow[0:1, :], ps_st[0:1, 0:NQ], 1.0 / D, None, ALU.mult))
            ve.wait_ge(mv.h, mv.n)
            mv.inc(ve.tensor_tensor(m2r[0:1, :], yrow[0:1, :], yrow[0:1, :], ALU.mult))
            ve.wait_ge(mv.h, mv.n)
            ve.scalar_tensor_tensor(yirow[0:1, :], ps_st[0:1, NQ:2 * NQ], 1.0 / D,
                                    m2r[0:1, :], ALU.mult, ALU.subtract)
            mv.inc(ve.tensor_scalar(yirow[0:1, :], yirow[0:1, :], EPS, None, ALU.add))
            istd_rows(yirow[0:1, :])

            xg_free = {0: 0, 1: 0}
            x1p_free = {0: 0, 1: 0}
            x1b_free = 0
            bc_busy = 0
            for c2 in range(2):
                te.wait_ge(ma.h, ma.n)
                te.wait_ge(mv.h, bc_busy)
                emit_bcast(ps_b2, 0, yrow[0:1, CH * c2:CH * (c2 + 1)], CH)
                emit_bcast(ps_b2, CH, yirow[0:1, CH * c2:CH * (c2 + 1)], CH)
                BC_MT = mt.n
                for grp in range(8):
                    gbb = grp % 2
                    sy.wait_ge(mv.h, xg_free[gbb])
                    dy.dma(sy.dma_start(
                        xg[gbb][:],
                        ap(xT_own, 128 * 4 * grp * NQ + CH * c2,
                           [[NQ, 128], [128 * NQ, 4], [1, CH]])))
                    XG_DY = dy.n
                    for a8 in range(4):
                        a = 4 * grp + a8
                        pp = a % 2
                        ve.wait_ge(mt.h, max(BC_MT, x1b_free))
                        if a8 == 0:
                            ve.wait_ge(dy.h, XG_DY)
                        ve.wait_ge(dy.h, x1p_free[pp])
                        tfb = tf2[a % 2]
                        ve.tensor_tensor(tfb[:], ysb[:, a, CH * c2:CH * (c2 + 1)],
                                         ps_b2[:, 0:CH], ALU.subtract)
                        ve.tensor_tensor(tfb[:], tfb[:], ps_b2[:, CH:2 * CH], ALU.mult)
                        mv.inc(ve.tensor_add(x1p[pp][:], tfb[:], xg[gbb][:, a8, :]))
                        ve.wait_ge(mv.h, mv.n)
                        mv.inc(ve.tensor_copy(x1b[:], x1p[pp][:]))
                        sy.wait_ge(mv.h, mv.n)
                        dy.dma(sy.dma_start(
                            ap(x1_scr, 128 * a * NQ + CH * c2, [[NQ, 128], [1, CH]]),
                            x1p[pp][:]))
                        x1p_free[pp] = dy.n
                        sc.wait_ge(mv.h, mv.n)
                        ma.inc(sc.activation(sqb[a % 2][:], x1b[:], AF.Square))
                        te.wait_ge(ma.h, ma.n)
                        te.matmul(ps_st[0:1, CH * c2:CH * (c2 + 1)],
                                  ones_col[:], x1b[:],
                                  start=(a == 0), stop=(a == ND - 1))
                        mt.inc(te.matmul(
                            ps_st[0:1, NQ + CH * c2:NQ + CH * (c2 + 1)],
                            ones_col[:], sqb[a % 2][:],
                            start=(a == 0), stop=(a == ND - 1)))
                        x1b_free = mt.n
                    xg_free[gbb] = mv.n
                bc_busy = mv.n
            X1_SPOOL_DY = dy.n
            ve.wait_ge(mt.h, mt.n)
            mv.inc(ve.tensor_scalar(yrow[0:1, :], ps_st[0:1, 0:NQ], 1.0 / D, None, ALU.mult))
            ve.wait_ge(mv.h, mv.n)
            mv.inc(ve.tensor_tensor(m2r[0:1, :], yrow[0:1, :], yrow[0:1, :], ALU.mult))
            ve.wait_ge(mv.h, mv.n)
            ve.scalar_tensor_tensor(yirow[0:1, :], ps_st[0:1, NQ:2 * NQ], 1.0 / D,
                                    m2r[0:1, :], ALU.mult, ALU.subtract)
            mv.inc(ve.tensor_scalar(yirow[0:1, :], yirow[0:1, :], EPS, None, ALU.add))
            istd_rows(yirow[0:1, :])

            bc_busy2 = 0
            for c2 in range(2):
                te.wait_ge(ma.h, ma.n)
                te.wait_ge(mv.h, bc_busy2)
                emit_bcast(ps_b2, 0, yrow[0:1, CH * c2:CH * (c2 + 1)], CH)
                emit_bcast(ps_b2, CH, yirow[0:1, CH * c2:CH * (c2 + 1)], CH)
                BC_MT = mt.n
                for grp in range(8):
                    gbb = grp % 2
                    sy.wait_ge(mv.h, xg_free[gbb])
                    sy.wait_ge(dy.h, X1_SPOOL_DY)
                    dy.dma(sy.dma_start(
                        xg[gbb][:],
                        ap(x1_scr, 128 * 4 * grp * NQ + CH * c2,
                           [[NQ, 128], [128 * NQ, 4], [1, CH]])))
                    XG_DY = dy.n
                    for a8 in range(4):
                        a = 4 * grp + a8
                        tfb = tf2[a % 2]
                        ve.wait_ge(mt.h, BC_MT)
                        if a8 == 0:
                            ve.wait_ge(dy.h, XG_DY)
                        ve.tensor_tensor(tfb[:], xg[gbb][:, a8, :],
                                         ps_b2[:, 0:CH], ALU.subtract)
                        mv.inc(ve.tensor_tensor(xn2[:, a, CH * c2:CH * (c2 + 1)],
                                                tfb[:], ps_b2[:, CH:2 * CH], ALU.mult))
                    xg_free[gbb] = mv.n
                bc_busy2 = mv.n

        if DEBUG_PHASE < 99 and DEBUG_PHASE >= 4:
            sy.wait_ge(mv.h, mv.n)
            dy.dma(sy.dma_start(dbg["X1"][:, :], x1_scr[:, :]))
            dy.dma(sy.dma_start(dbg["XN2"][:, :],
                                xn2[:].rearrange("p a c -> p (a c)")))
        if DEBUG_PHASE == 4:
            sy.wait_ge(dy.h, dy.n)
            sy.wait_ge(dg.h, dg.n)
            sy.wait_ge(dc.h, dc.n)
            return nc, dbg

        fence()
        with ExitStack() as pg:
            Ep = pg.enter_context
            a1 = Ep(nc.sbuf_tensor("a1", [128, ND, R], BF16))
            a3 = Ep(nc.sbuf_tensor("a3", [128, ND, R], BF16))
            b1 = Ep(nc.sbuf_tensor("b1", [R, HFF], BF16))
            b3 = Ep(nc.sbuf_tensor("b3", [R, HFF], BF16))
            t1 = Ep(nc.sbuf_tensor("t1", [R, NQ], BF16))
            t3 = Ep(nc.sbuf_tensor("t3", [R, NQ], BF16))
            w1p = [Ep(nc.sbuf_tensor(f"w1p{i}", [128, ND, 128], BF16)) for i in range(2)]
            w3p = [Ep(nc.sbuf_tensor(f"w3p{i}", [128, ND, 128], BF16)) for i in range(2)]
            s1 = Ep(nc.sbuf_tensor("s1", [128, NQ], BF16))
            gt = [Ep(nc.sbuf_tensor(f"gt{i}", [128, NQ], BF16)) for i in range(2)]
            ps_h1 = [Ep(nc.psum_tensor(f"ps_h1{i}", [128, NQ], F32)) for i in range(2)]
            ps_h3 = [Ep(nc.psum_tensor(f"ps_h3{i}", [128, NQ], F32)) for i in range(2)]

            dg.dma(gp.dma_start(a1[:], ap(A1T, 0, [[R, 128], [128 * R, ND], [1, R]])))
            dg.dma(gp.dma_start(a3[:], ap(A3T, 0, [[R, 128], [128 * R, ND], [1, R]])))
            dg.dma(gp.dma_start(b1[:], B1T[:, :]))
            dg.dma(gp.dma_start(b3[:], B3T[:, :]))
            te.wait_ge(dg.h, dg.n)
            te.wait_ge(mv.h, mv.n)
            for (aa, tt, pbank) in ((a1, t1, ps_h1[0]), (a3, t3, ps_h3[0])):
                ins = None
                for c2 in range(2):
                    for a in range(ND):
                        ins = te.matmul(pbank[0:R, CH * c2:CH * (c2 + 1)],
                                        aa[:, a, :], xn2[:, a, CH * c2:CH * (c2 + 1)],
                                        start=(a == 0), stop=(a == ND - 1))
                mt.inc(ins)
                ve.wait_ge(mt.h, mt.n)
                mv.inc(ve.tensor_scalar(tt[:, :], pbank[0:R, :], 2.0, None, ALU.mult))
            T13_MV = mv.n

            w1p_free = {0: 0, 1: 0}
            s1_busy = 0
            bank_free_a = {0: 0, 1: 0}
            bank_free_v = {0: T13_MV, 1: T13_MV}
            gt_free = {0: 0, 1: 0}
            for h in range(NHT):
                pb = h % 2
                gp.wait_ge(mt.h, w1p_free[pb])
                dg.dma(gp.dma_start(
                    w1p[pb][:],
                    ap(w1T, 128 * h, [[HFF, 128], [128 * HFF, ND], [1, 128]])))
                dg.dma(gp.dma_start(
                    w3p[pb][:],
                    ap(w3T, 128 * h, [[HFF, 128], [128 * HFF, ND], [1, 128]])))
                bank = h % 2
                te.wait_ge(dg.h, dg.n)
                te.wait_ge(ma.h, bank_free_a[bank])
                te.wait_ge(mv.h, bank_free_v[bank])
                ins = None
                for c2 in range(2):
                    for a in range(ND):
                        te.matmul(ps_h1[bank][:, CH * c2:CH * (c2 + 1)],
                                  w1p[pb][:, a, :], xn2[:, a, CH * c2:CH * (c2 + 1)],
                                  start=(a == 0), stop=False)
                    te.matmul(ps_h1[bank][:, CH * c2:CH * (c2 + 1)],
                              b1[0:R, 128 * h:128 * (h + 1)],
                              t1[0:R, CH * c2:CH * (c2 + 1)], start=False, stop=True)
                    for a in range(ND):
                        te.matmul(ps_h3[bank][:, CH * c2:CH * (c2 + 1)],
                                  w3p[pb][:, a, :], xn2[:, a, CH * c2:CH * (c2 + 1)],
                                  start=(a == 0), stop=False)
                    ins = te.matmul(ps_h3[bank][:, CH * c2:CH * (c2 + 1)],
                                    b3[0:R, 128 * h:128 * (h + 1)],
                                    t3[0:R, CH * c2:CH * (c2 + 1)],
                                    start=False, stop=True)
                mt.inc(ins)
                w1p_free[pb] = mt.n
                sc.wait_ge(mt.h, mt.n)
                sc.wait_ge(mv.h, s1_busy)
                ma.inc(sc.activation(s1[:], ps_h1[bank][:, :], AF.Silu))
                bank_free_a[bank] = ma.n
                gb = h % 2
                ve.wait_ge(ma.h, ma.n)
                ve.wait_ge(dc.h, gt_free[gb])
                mv.inc(ve.tensor_tensor(gt[gb][:], s1[:], ps_h3[bank][:, :], ALU.mult))
                s1_busy = mv.n
                bank_free_v[bank] = mv.n
                sc.wait_ge(mv.h, mv.n)
                dc.dma(sc.dma_start(
                    ap(g_scr, 128 * h * NQ, [[NQ, 128], [1, NQ]]), gt[gb][:]))
                gt_free[gb] = dc.n

        if DEBUG_PHASE < 99 and DEBUG_PHASE >= 5:
            sy.wait_ge(dc.h, dc.n)
            dy.dma(sy.dma_start(dbg["G"][:, :], g_scr[:, :]))
        if DEBUG_PHASE == 5:
            sy.wait_ge(dy.h, dy.n)
            sy.wait_ge(dg.h, dg.n)
            sy.wait_ge(dc.h, dc.n)
            return nc, dbg

        fence()
        xn2_stack.close()
        with ExitStack() as p2:
            Ep = p2.enter_context
            a2 = Ep(nc.sbuf_tensor("a2", [128, NHT, R], BF16))
            b2 = Ep(nc.sbuf_tensor("b2", [R, D], BF16))
            t2 = Ep(nc.sbuf_tensor("t2", [R, NQ], BF16))
            gs = [Ep(nc.sbuf_tensor(f"gs{i}", [128, 2, NQ], BF16)) for i in range(2)]
            w2h = [Ep(nc.sbuf_tensor(f"w2h{i}", [128, 43, CH], BF16)) for i in range(2)]
            ffs = [Ep(nc.sbuf_tensor(f"ffs{i}", [128, CH], BF16)) for i in range(2)]

            dg.dma(gp.dma_start(a2[:], ap(A2T, 0, [[R, 128], [128 * R, NHT], [1, R]])))
            dg.dma(gp.dma_start(b2[:], B2T[:, :]))
            PW2_CONST_DG = dg.n
            with ExitStack() as p2t:
                ps_t2 = p2t.enter_context(nc.psum_tensor("ps_t2", [R, NQ], F32))
                gs_free = {0: 0, 1: 0}
                te.wait_ge(dg.h, PW2_CONST_DG)
                sy.wait_ge(dc.h, dc.n)
                ins = None
                for hp in range(NHT // 2):
                    gb = hp % 2
                    sy.wait_ge(mt.h, gs_free[gb])
                    dy.dma(sy.dma_start(
                        gs[gb][:],
                        ap(g_scr, 256 * hp * NQ, [[NQ, 128], [128 * NQ, 2], [1, NQ]])))
                    te.wait_ge(dy.h, dy.n)
                    for h2 in range(2):
                        h = 2 * hp + h2
                        for c2 in range(2):
                            ins = te.matmul(ps_t2[0:R, CH * c2:CH * (c2 + 1)],
                                            a2[:, h, :],
                                            gs[gb][:, h2, CH * c2:CH * (c2 + 1)],
                                            start=(h == 0), stop=(h == NHT - 1))
                    mt.inc(ins)
                    gs_free[gb] = mt.n
                ve.wait_ge(mt.h, mt.n)
                mv.inc(ve.tensor_scalar(t2[:, :], ps_t2[0:R, :], 2.0, None, ALU.mult))
            T2_MV = mv.n

            ps_w = [Ep(nc.psum_tensor(f"ps_w{i}", [128, NQ], F32)) for i in range(4)]
            w2h_free = {0: 0, 1: 0}
            w2h_dg = {0: 0, 1: 0}
            evac_ma = 0
            ffs_free = {0: 0, 1: 0}

            def load_w2h(slot, s8_, half_):
                gp.wait_ge(mt.h, w2h_free[slot])
                dg.dma(gp.dma_start(
                    w2h[slot][:],
                    ap(w2T, 512 * s8_ + (43 * 128 * D) * half_,
                       [[D, 128], [128 * D, 43], [1, CH]])))
                w2h_dg[slot] = dg.n

            load_w2h(0, 0, 0)
            load_w2h(1, 0, 1)
            for s8 in range(8):
                gs_free = {0: mt.n, 1: mt.n}
                for hp in range(NHT // 2):
                    gb = hp % 2
                    sy.wait_ge(mt.h, gs_free[gb])
                    dy.dma(sy.dma_start(
                        gs[gb][:],
                        ap(g_scr, 256 * hp * NQ, [[NQ, 128], [128 * NQ, 2], [1, NQ]])))
                    te.wait_ge(dy.h, dy.n)
                    if hp == 0:
                        te.wait_ge(dg.h, w2h_dg[0])
                        te.wait_ge(ma.h, evac_ma)
                        te.wait_ge(mv.h, T2_MV)
                    if hp == 22:
                        te.wait_ge(dg.h, w2h_dg[1])
                    ins = None
                    for h2 in range(2):
                        h = 2 * hp + h2
                        slot = 0 if h < 43 else 1
                        hloc = h if h < 43 else h - 43
                        for d4 in range(4):
                            for c2 in range(2):
                                ins = te.matmul(
                                    ps_w[d4][:, CH * c2:CH * (c2 + 1)],
                                    w2h[slot][:, hloc, 128 * d4:128 * (d4 + 1)],
                                    gs[gb][:, h2, CH * c2:CH * (c2 + 1)],
                                    start=(h == 0), stop=False)
                    mt.inc(ins)
                    gs_free[gb] = mt.n
                    if hp == 21:
                        w2h_free[0] = mt.n
                        if s8 < 7:
                            load_w2h(0, s8 + 1, 0)
                ins = None
                for d4 in range(4):
                    for c2 in range(2):
                        ins = te.matmul(ps_w[d4][:, CH * c2:CH * (c2 + 1)],
                                        b2[0:R, 512 * s8 + 128 * d4:512 * s8 + 128 * (d4 + 1)],
                                        t2[0:R, CH * c2:CH * (c2 + 1)],
                                        start=False, stop=True)
                mt.inc(ins)
                w2h_free[1] = mt.n
                if s8 < 7:
                    load_w2h(1, s8 + 1, 1)
                sc.wait_ge(mt.h, mt.n)
                for d4 in range(4):
                    for c2 in range(2):
                        fb = (2 * d4 + c2) % 2
                        sc.wait_ge(dc.h, ffs_free[fb])
                        ma.inc(sc.activation(ffs[fb][:],
                                             ps_w[d4][:, CH * c2:CH * (c2 + 1)],
                                             AF.Copy))
                        dc.dma(sc.dma_start(
                            ap(ffn_scr, (512 * s8 + 128 * d4) * NQ + CH * c2,
                               [[NQ, 128], [1, CH]]), ffs[fb][:]))
                        ffs_free[fb] = dc.n
                evac_ma = ma.n

        fence()
        with ExitStack() as pf:
            Ep = pf.enter_context
            ff = Ep(nc.sbuf_tensor("ff", [128, ND, NQ], BF16))
            sqb = [Ep(nc.sbuf_tensor(f"fsq{i}", [128, CH], BF16)) for i in range(2)]
            frow = Ep(nc.sbuf_tensor("frow", [1, NQ], F32))
            firow = Ep(nc.sbuf_tensor("firow", [1, NQ], F32))
            m2r = Ep(nc.sbuf_tensor("fm2r", [1, NQ], F32))
            xg = [Ep(nc.sbuf_tensor(f"fxg{i}", [128, 4, CH], F32)) for i in range(2)]
            tf2 = [Ep(nc.sbuf_tensor(f"ftf{i}", [128, CH], F32)) for i in range(2)]
            outp = [Ep(nc.sbuf_tensor(f"outp{i}", [128, CH], F32)) for i in range(2)]
            ps_st = Ep(nc.psum_tensor("fps_st", [1, 2 * NQ], F32))
            ps_b3 = Ep(nc.psum_tensor("ps_b3", [128, 2 * CH], F32))

            sy.wait_ge(dc.h, dc.n)
            dy.dma(sy.dma_start(
                ff[:], ap(ffn_scr, 0, [[NQ, 128], [128 * NQ, ND], [1, NQ]])))
            FF_DY = dy.n
            sq_free = {0: 0, 1: 0}
            for a in range(ND):
                for c2 in range(2):
                    sb = (2 * a + c2) % 2
                    sc.wait_ge(mt.h, sq_free[sb])
                    if a == 0 and c2 == 0:
                        sc.wait_ge(dy.h, FF_DY)
                        te.wait_ge(dy.h, FF_DY)
                    ma.inc(sc.activation(sqb[sb][:], ff[:, a, CH * c2:CH * (c2 + 1)],
                                         AF.Square))
                    te.wait_ge(ma.h, ma.n)
                    te.matmul(ps_st[0:1, CH * c2:CH * (c2 + 1)], ones_col[:],
                              ff[:, a, CH * c2:CH * (c2 + 1)],
                              start=(a == 0), stop=(a == ND - 1))
                    mt.inc(te.matmul(ps_st[0:1, NQ + CH * c2:NQ + CH * (c2 + 1)],
                                     ones_col[:], sqb[sb][:],
                                     start=(a == 0), stop=(a == ND - 1)))
                    sq_free[sb] = mt.n
            ve.wait_ge(mt.h, mt.n)
            mv.inc(ve.tensor_scalar(frow[0:1, :], ps_st[0:1, 0:NQ], 1.0 / D, None, ALU.mult))
            ve.wait_ge(mv.h, mv.n)
            mv.inc(ve.tensor_tensor(m2r[0:1, :], frow[0:1, :], frow[0:1, :], ALU.mult))
            ve.wait_ge(mv.h, mv.n)
            ve.scalar_tensor_tensor(firow[0:1, :], ps_st[0:1, NQ:2 * NQ], 1.0 / D,
                                    m2r[0:1, :], ALU.mult, ALU.subtract)
            mv.inc(ve.tensor_scalar(firow[0:1, :], firow[0:1, :], EPS, None, ALU.add))
            istd_rows(firow[0:1, :])
            outp_free = {0: 0, 1: 0}
            xg_free = {0: 0, 1: 0}
            bc_busy = 0
            for c2 in range(2):
                te.wait_ge(ma.h, ma.n)
                te.wait_ge(mv.h, bc_busy)
                emit_bcast(ps_b3, 0, frow[0:1, CH * c2:CH * (c2 + 1)], CH)
                emit_bcast(ps_b3, CH, firow[0:1, CH * c2:CH * (c2 + 1)], CH)
                BC_MT = mt.n
                for grp in range(8):
                    gbb = grp % 2
                    sy.wait_ge(mv.h, xg_free[gbb])
                    dy.dma(sy.dma_start(
                        xg[gbb][:],
                        ap(x1_scr, 128 * 4 * grp * NQ + CH * c2,
                           [[NQ, 128], [128 * NQ, 4], [1, CH]])))
                    XG_DY = dy.n
                    for a8 in range(4):
                        a = 4 * grp + a8
                        ob = a % 2
                        ve.wait_ge(mt.h, BC_MT)
                        if a8 == 0:
                            ve.wait_ge(dy.h, XG_DY)
                        ve.wait_ge(dy.h, outp_free[ob])
                        tfb = tf2[a % 2]
                        ve.tensor_tensor(tfb[:], ff[:, a, CH * c2:CH * (c2 + 1)],
                                         ps_b3[:, 0:CH], ALU.subtract)
                        ve.tensor_tensor(tfb[:], tfb[:], ps_b3[:, CH:2 * CH], ALU.mult)
                        mv.inc(ve.tensor_add(outp[ob][:], tfb[:], xg[gbb][:, a8, :]))
                        sy.wait_ge(mv.h, mv.n)
                        dy.dma(sy.dma_start(
                            ap(y_out, 128 * a * NQ + CH * c2, [[NQ, 128], [1, CH]]),
                            outp[ob][:]))
                        outp_free[ob] = dy.n
                    xg_free[gbb] = mv.n
                bc_busy = mv.n

        if DEBUG_PHASE < 99 and DEBUG_PHASE >= 6:
            dy.dma(sy.dma_start(dbg["FF"][:, :], ffn_scr[:, :]))
        sy.wait_ge(dy.h, dy.n)
        sy.wait_ge(dg.h, dg.n)
        sy.wait_ge(dc.h, dc.n)
      except _Cut:
        sy.wait_ge(dy.h, dy.n)
        sy.wait_ge(dg.h, dg.n)
        sy.wait_ge(dc.h, dc.n)
        ve.wait_ge(dy.h, dy.n)
        te.wait_ge(dy.h, dy.n)
        sc.wait_ge(dy.h, dy.n)
    finally:
        for s in (qt_stack, ab_stack, xn2_stack):
            try:
                s.close()
            except Exception:
                pass
    return nc, dbg


_CACHE = {}


def _tables():
    inv = 1.0 / (10000.0 ** (np.arange(0, 128, 2, dtype=np.float32) / 128.0))
    t = np.arange(S, dtype=np.float32)
    fr = np.outer(t, inv)
    cos = np.cos(fr).astype(np.float32).T
    sin = np.sin(fr).astype(np.float32).T
    cosT = np.ascontiguousarray(np.concatenate([cos, cos], axis=0))
    sinT = np.ascontiguousarray(np.concatenate([sin, sin], axis=0))
    return cosT, sinT


def make_inputs(ins):
    C = np.ascontiguousarray
    cosT, sinT = _tables()
    x = np.asarray(ins["L0_attn_hidden_states"], dtype=np.float32)
    shared = {
        "WqT": C(np.asarray(ins["L0_attn_Wq"]).T),
        "WkT": C(np.asarray(ins["L0_attn_Wk"]).T),
        "WvT": C(np.asarray(ins["L0_attn_Wv"]).T),
        "WoT": C(np.asarray(ins["L0_attn_Wo"]).T),
        "w1T": C(np.asarray(ins["L0_ffn_w1"]).T),
        "w3T": C(np.asarray(ins["L0_ffn_w3"]).T),
        "w2T": C(np.asarray(ins["L0_ffn_w2"]).T),
        "A1T": C(np.asarray(ins["L0_ffn_lora_A1"]).T),
        "A3T": C(np.asarray(ins["L0_ffn_lora_A3"]).T),
        "A2T": C(np.asarray(ins["L0_ffn_lora_A2"]).T),
        "B1T": C(np.asarray(ins["L0_ffn_lora_B1"]).T),
        "B3T": C(np.asarray(ins["L0_ffn_lora_B3"]).T),
        "B2T": C(np.asarray(ins["L0_ffn_lora_B2"]).T),
        "cosT": cosT, "sinT": sinT,
    }
    in_maps = []
    own_rows = []
    for c in range(8):
        b, j = divmod(c, 4)
        xb = x[b]
        xbT = C(xb.T)
        rows = list(range(CH * j, CH * (j + 1))) + list(range(CH * (7 - j), CH * (8 - j)))
        own_rows.append((b, rows))
        m = dict(shared)
        m["x_tm"] = xb
        m["xT"] = xbT
        m["x_own_tm"] = C(xb[rows])
        m["xT_own"] = C(xbT[:, rows])
        m["cos_q"] = C(cosT[:, rows])
        m["sin_q"] = C(sinT[:, rows])
        in_maps.append(m)
    return in_maps, own_rows


def kernel(**ins):
    if "nc" not in _CACHE:
        _CACHE["nc"] = build_nc()
    nc, dbg = _CACHE["nc"]
    in_maps, own_rows = make_inputs(ins)
    res = run_bass_kernel_spmd(nc, in_maps, core_ids=list(range(8)))
    _CACHE["res"] = res
    out = np.zeros((2, S, D), dtype=np.float32)
    for c in range(8):
        b, rows = own_rows[c]
        out[b, rows, :] = res.results[c]["y_out"].T
    return out
